# version 02
# baseline (speedup 1.0000x reference)
import contextlib
import numpy as np
import ml_dtypes
import concourse.bass as bass
import concourse.mybir as mybir
from concourse.bass_utils import run_bass_kernel_spmd
from concourse.alu_op_type import AluOpType as OP

F32 = mybir.dt.float32
BF16 = mybir.dt.bfloat16
AF = mybir.ActivationFunctionType
AX = mybir.AxisListType

T = 2048
D = 2048
NT = 16
ND = 16
HEADS = 16
HD = 128
RANK = 16
SCALING = 0.5
SCALE = HD ** -0.5
DFF = 8192
NF = 64
MOE_H = 1024
NE = 8
LN_EPS = 1e-5
NORM_EPS = 1e-12

ENGS = ("pe", "act", "dve", "pool", "sp")
SEMS = ("pe", "act", "dve", "pool", "qsp", "qpool")

BF = ml_dtypes.bfloat16



def _t(a):
    return np.ascontiguousarray(np.asarray(a).T)


def _cols(vec):
    return np.ascontiguousarray(np.asarray(vec).reshape(-1, 128).T)


def prep_inputs(inputs):
    w = {}
    for p in ("q", "k", "v", "o"):
        w[f"l0_w{p}T"] = _t(inputs[f"L0_attn_W{p}"])
        w[f"l0_B{p}T"] = _t(inputs[f"L0_attn_B{p}"])
        w[f"l0_A{p}T"] = _t(inputs[f"L0_attn_A{p}"])
    w["l0_bq"] = _cols(inputs["L0_attn_bq"])
    w["l0_bk"] = _cols(inputs["L0_attn_bk"])
    w["l0_bo"] = _cols(inputs["L0_attn_bo"])
    w["l0_bv_row"] = np.ascontiguousarray(np.asarray(inputs["L0_attn_bv"]).reshape(1, D))
    w["l0_w1T"] = _t(inputs["L0_ffn_w1"])
    w["l0_w2T"] = _t(inputs["L0_ffn_w2"])
    w["l0_b1"] = _cols(inputs["L0_ffn_b1"])
    w["l0_b2"] = _cols(inputs["L0_ffn_b2"])
    for p, nm in (("q", "Wq"), ("k", "Wk"), ("v", "Wv"), ("o", "Wo"), ("g", "gate_w")):
        w[f"l1_w{p}T"] = _t(inputs[f"L1_attn_{nm}"])
    w["l1_conv_w"] = np.ascontiguousarray(
        np.asarray(inputs["L1_attn_conv_w"]).reshape(D, 4))
    w["l1_conv_b"] = _cols(inputs["L1_attn_conv_b"])
    w["l1_gate_b"] = _cols(inputs["L1_attn_gate_b"])
    w["l1_ln_w"] = _cols(inputs["L1_attn_ln_w"])
    w["l1_ln_b"] = _cols(inputs["L1_attn_ln_b"])
    gam = np.asarray(inputs["L1_attn_gamma"]).reshape(HEADS)
    w["l1_gamma_cols"] = np.ascontiguousarray(
        np.repeat(gam.reshape(1, HEADS), 128, axis=0))
    w["moe_gateT"] = _t(inputs["L1_ffn_gate_w"])
    w["moe_w1T"] = np.ascontiguousarray(
        np.transpose(np.asarray(inputs["L1_ffn_w1"]), (0, 2, 1)))
    w["moe_w3T"] = np.ascontiguousarray(
        np.transpose(np.asarray(inputs["L1_ffn_w3"]), (0, 2, 1)))
    w["moe_w2T"] = np.ascontiguousarray(
        np.transpose(np.asarray(inputs["L1_ffn_w2"]), (0, 2, 1)))
    w["moe_sw1T"] = _t(inputs["L1_ffn_sw1"])
    w["moe_sw3T"] = _t(inputs["L1_ffn_sw3"])
    w["moe_sw2T"] = _t(inputs["L1_ffn_sw2"])
    w["c_id"] = np.eye(128, dtype=np.float32)
    w["c_ones_col"] = np.ones((128, 1), np.float32)
    w["c_ones_row"] = np.ones((1, 128), np.float32)

    x = np.asarray(inputs["L0_attn_x"])
    maps = []
    for c in range(8):
        m = dict(w)
        m["x"] = np.ascontiguousarray(x[c])
        maps.append(m)
    return maps



class KB:

    def __init__(self, nc):
        self.nc = nc
        self.ops = {e: [] for e in ENGS}
        self.cnt = {s: 0 for s in SEMS}
        self.S = None

    def v(self, s):
        return self.cnt[s]

    def emit(self, eng, fn, waits=(), inc=None, incn=1):
        tok = None
        if inc is not None:
            self.cnt[inc] += incn
            tok = (inc, self.cnt[inc])
        waits = tuple(waits)
        S = lambda: self.S

        def cl(e, _fn=fn, _w=waits, _inc=inc, _incn=incn):
            for (s, v) in _w:
                e.wait_ge(self.S[s], v)
            r = _fn(e)
            if _inc is not None:
                r.then_inc(self.S[_inc], _incn)
        self.ops[eng].append(cl)
        return tok

    def dma(self, queue, out, in_, waits=(), **kw):
        sem = {"sp": "qsp", "pool": "qpool"}[queue]
        return self.emit(queue, lambda e: e.dma_start(out, in_, **kw),
                         waits=waits, inc=sem, incn=16)

    def xpose(self, out, in_, waits=()):
        return self.emit("sp", lambda e: e.dma_start_transpose(out, in_),
                         waits=waits, inc="qsp", incn=16)


def alloc(nc, st):
    t = {}
    dram_in = [
        ("x", [T, D], F32),
        ("l0_wqT", [D, D], F32), ("l0_wkT", [D, D], F32),
        ("l0_wvT", [D, D], F32), ("l0_woT", [D, D], F32),
        ("l0_BqT", [D, RANK], F32), ("l0_BkT", [D, RANK], F32),
        ("l0_BvT", [D, RANK], F32), ("l0_BoT", [D, RANK], F32),
        ("l0_AqT", [RANK, D], F32), ("l0_AkT", [RANK, D], F32),
        ("l0_AvT", [RANK, D], F32), ("l0_AoT", [RANK, D], F32),
        ("l0_bq", [128, ND], F32), ("l0_bk", [128, ND], F32),
        ("l0_bo", [128, ND], F32), ("l0_bv_row", [1, D], F32),
        ("l0_w1T", [D, DFF], F32), ("l0_w2T", [DFF, D], F32),
        ("l0_b1", [128, NF], F32), ("l0_b2", [128, ND], F32),
        ("l1_wqT", [D, D], F32), ("l1_wkT", [D, D], F32),
        ("l1_wvT", [D, D], F32), ("l1_woT", [D, D], F32),
        ("l1_wgT", [D, D], F32),
        ("l1_conv_w", [D, 4], F32), ("l1_conv_b", [128, ND], F32),
        ("l1_gate_b", [128, ND], F32),
        ("l1_ln_w", [128, ND], F32), ("l1_ln_b", [128, ND], F32),
        ("l1_gamma_cols", [128, HEADS], F32),
        ("moe_gateT", [D, NE], F32),
        ("moe_w1T", [NE, D, MOE_H], F32), ("moe_w3T", [NE, D, MOE_H], F32),
        ("moe_w2T", [NE, MOE_H, D], F32),
        ("moe_sw1T", [D, MOE_H], F32), ("moe_sw3T", [D, MOE_H], F32),
        ("moe_sw2T", [MOE_H, D], F32),
        ("c_id", [128, 128], F32),
        ("c_ones_col", [128, 1], F32), ("c_ones_row", [1, 128], F32),
    ]
    for nm, sh, dt in dram_in:
        t[nm] = nc.dram_tensor(nm, sh, dt, kind="ExternalInput").ap()
    t["out"] = nc.dram_tensor("out", [T, D], F32, kind="ExternalOutput").ap()

    scratch = [
        ("sc_x0d", [128, ND, T], BF16),
        ("sc_o", [128, ND, T], BF16),
        ("sc_s0", [128, ND, T], BF16),
        ("sc_x1", [128, ND, T], BF16),
        ("sc_xn1", [128, ND, T], BF16),
        ("sc_gate", [128, ND, T], BF16),
        ("sc_y", [128, ND, T], BF16),
        ("sc_s1", [128, ND, T], BF16),
        ("sc_w1q", [ND, 128, DFF], BF16),
        ("sc_w2q", [NF, 128, D], BF16),
        ("sc_wem", [NE, T], F32),
        ("sc_x2d", [128, ND, T], BF16),
    ]
    for nm, sh, dt in scratch:
        kind = "ExternalOutput" if nm in alloc.dump else "Internal"
        t[nm] = nc.dram_tensor(nm, sh, dt, kind=kind).ap()

    E = st.enter_context
    t["XN"] = E(nc.sbuf_tensor("XN", [128, ND * T], BF16))
    t["BIG"] = E(nc.sbuf_tensor("BIG", [128, ND * T], BF16))
    t["SMALL"] = E(nc.sbuf_tensor("SMALL", [128, 8 * T], BF16))
    t["POOLT"] = E(nc.sbuf_tensor("POOLT", [128, 4 * T], BF16))
    t["STREAM"] = E(nc.sbuf_tensor("STREAM", [128, 3 * T - 1024], BF16))
    t["id_bf"] = E(nc.sbuf_tensor("id_bf", [128, 128], BF16))
    t["id_f32"] = E(nc.sbuf_tensor("id_f32", [128, 128], F32))
    t["ones_col_bf"] = E(nc.sbuf_tensor("ones_col_bf", [128, 1], BF16))
    t["ones_row_bf"] = E(nc.sbuf_tensor("ones_row_bf", [1, 128], BF16))
    t["ones_row_f32"] = E(nc.sbuf_tensor("ones_row_f32", [1, 128], F32))
    t["ones_col_f32"] = E(nc.sbuf_tensor("ones_col_f32", [128, 1], F32))
    for nm in ("l0_bq", "l0_bk", "l0_bo", "l0_b2", "l1_conv_b", "l1_gate_b",
               "l1_ln_w", "l1_ln_b", "l1_gamma_cols"):
        t["s_" + nm] = E(nc.sbuf_tensor("s_" + nm, [128, ND], F32))
    t["s_l0_b1"] = E(nc.sbuf_tensor("s_l0_b1", [128, NF], F32))
    t["s_l0_bv_row"] = E(nc.sbuf_tensor("s_l0_bv_row", [1, D], BF16))
    t["s_conv_w"] = E(nc.sbuf_tensor("s_conv_w", [128, ND * 4], F32))
    for p in ("q", "k", "v", "o"):
        t[f"s_B{p}T"] = E(nc.sbuf_tensor(f"s_B{p}T", [128, ND * RANK], BF16))
    for p in ("q", "k", "v"):
        t[f"s_u{p}"] = E(nc.sbuf_tensor(f"s_u{p}", [RANK, D], BF16))
    t["s_gateT"] = E(nc.sbuf_tensor("s_gateT", [128, ND * NE], BF16))
    t["s_stat"] = E(nc.sbuf_tensor("s_stat", [128, 80], F32))
    for i in range(8):
        t[f"PB{i}"] = E(nc.psum_tensor(f"PB{i}", [128, 512], F32))
    return t



def dmaj(buf, d, sl=None):
    if sl is None:
        return buf[:, d * T:(d + 1) * T]
    return buf[:, d * T + sl.start: d * T + sl.stop]


def emit_consts(k, t):
    k.dma("pool", t["id_bf"][:], t["c_id"])
    k.dma("sp", t["id_f32"][:], t["c_id"])
    k.dma("pool", t["ones_col_bf"][:], t["c_ones_col"])
    k.dma("pool", t["ones_row_bf"][:], t["c_ones_row"])
    k.dma("sp", t["ones_row_f32"][:], t["c_ones_row"])
    k.dma("sp", t["ones_col_f32"][:], t["c_ones_col"])
    for nm in ("l0_bq", "l0_bk", "l0_bo", "l0_b2", "l1_conv_b", "l1_gate_b",
               "l1_ln_w", "l1_ln_b"):
        k.dma("sp", t["s_" + nm][:], t[nm])
    k.dma("sp", t["s_l1_gamma_cols"][:], t["l1_gamma_cols"])
    k.dma("sp", t["s_l0_b1"][:], t["l0_b1"])
    k.dma("pool", t["s_l0_bv_row"][:], t["l0_bv_row"])
    k.dma("sp", t["s_conv_w"].ap().rearrange("p (a k) -> p a k", a=ND),
          t["l1_conv_w"].rearrange("(a p) k -> p a k", p=128))
    for p in ("q", "k", "v", "o"):
        k.dma("pool", t[f"s_B{p}T"].ap().rearrange("p (a r) -> p a r", a=ND),
              t[f"l0_B{p}T"].rearrange("(a p) r -> p a r", p=128))
    k.dma("pool", t["s_gateT"].ap().rearrange("p (a e) -> p a e", a=ND),
          t["moe_gateT"].rearrange("(a p) e -> p a e", p=128))
    return k.v("qsp"), k.v("qpool")


def emit_phaseA(k, t):
    BIGf = t["BIG"].ap().bitcast(F32)
    XT = [BIGf[:, i * D:(i + 1) * D] for i in range(2)]
    SQ = [BIGf[:, (2 + i) * D:(3 + i) * D] for i in range(2)]
    SM = t["SMALL"]
    XNT = [SM[:, i * D:(i + 1) * D] for i in range(2)]
    X0T = [SM[:, (2 + i) * D:(3 + i) * D] for i in range(2)]
    X0S = [SM[:, (4 + i) * D:(5 + i) * D] for i in range(2)]
    ST = t["s_stat"].ap()

    xt_free = [[], []]
    xnt_free = [[], []]
    x0t_free = [[], []]
    x0s_free = [[], []]
    xn_view = t["XN"].ap().rearrange("p (m tt) -> p m tt", m=ND)
    x0d = t["sc_x0d"]
    for i in range(NT):
        b = i % 2
        ld = k.dma("sp", XT[b], t["x"][i * 128:(i + 1) * 128, :], waits=xt_free[b])
        sx, ssq = ST[:, b * 8:b * 8 + 1], ST[:, b * 8 + 1:b * 8 + 2]
        mu = ST[:, b * 8 + 2:b * 8 + 3]
        var, sd = ST[:, b * 8 + 4:b * 8 + 5], ST[:, b * 8 + 5:b * 8 + 6]
        rstd = ST[:, b * 8 + 6:b * 8 + 7]
        tk_sq = k.emit("act", lambda e, x=XT[b], o=SQ[b], s=ssq:
                       e.activation(o, x, AF.Square, accum_out=s),
                       waits=[ld], inc="act")
        tk_sx = k.emit("dve", lambda e, x=XT[b], s=sx:
                       e.reduce_sum(s, x, axis=AX.X), waits=[ld], inc="dve")
        v1 = k.emit("dve", lambda e, sx=sx, mu=mu: e.tensor_scalar(
            mu, sx, 1.0 / D, None, OP.mult, OP.bypass),
            waits=[tk_sq, tk_sx], inc="dve")
        v2 = k.emit("dve", lambda e, ssq=ssq, var=var: e.tensor_scalar(
            var, ssq, 1.0 / D, None, OP.mult, OP.bypass), waits=[v1], inc="dve")
        v3 = k.emit("dve", lambda e, sx=sx, mu=mu: e.tensor_tensor(
            sx, mu, mu, OP.mult), waits=[v2], inc="dve")
        v4 = k.emit("dve", lambda e, sx=sx, var=var: e.tensor_tensor(
            var, var, sx, OP.subtract), waits=[v3], inc="dve")
        tk_var = k.emit("dve", lambda e, var=var: e.tensor_scalar(
            var, var, LN_EPS, None, OP.add, OP.bypass), waits=[v4], inc="dve")
        tk_sd = k.emit("act", lambda e, var=var, sd=sd:
                       e.sqrt(sd, var), waits=[tk_var], inc="act")
        tk_rs = k.emit("dve", lambda e, sd=sd, rstd=rstd:
                       e.reciprocal(rstd, sd), waits=[tk_sd], inc="dve")
        tk_xn = k.emit("dve", lambda e, x=XT[b], o=XNT[b], mu=mu, rstd=rstd:
                       e.tensor_scalar(o, x, mu, rstd, OP.subtract, OP.mult),
                       waits=[tk_rs] + xnt_free[b], inc="dve")
        tk_x0 = k.emit("act", lambda e, x=XT[b], o=X0T[b]:
                       e.copy(o, x), waits=[ld] + x0t_free[b], inc="act")
        xt_free[b] = [tk_xn, tk_x0]
        xp1 = k.xpose(xn_view[:, :, i * 128:(i + 1) * 128], XNT[b], waits=[tk_xn])
        xnt_free[b] = [xp1]
        xp2 = k.xpose(X0S[b].rearrange("p (m tt) -> p m tt", m=ND),
                      X0T[b], waits=[tk_x0] + x0s_free[b])
        x0t_free[b] = [xp2]
        st_tok = k.dma("sp", x0d[:, :, i * 128:(i + 1) * 128],
                       X0S[b].rearrange("p (m tt) -> p m tt", m=ND), waits=[xp2])
        x0s_free[b] = [st_tok]


def emit_phaseB(k, t, groups=range(4), heads=range(4)):
    XN, BIG, SM, PT, STR = (t[n] for n in ("XN", "BIG", "SMALL", "POOLT", "STREAM"))
    PB = [t[f"PB{i}"] for i in range(8)]
    ST = t["s_stat"].ap()
    PTf = PT.ap().bitcast(F32)
    onec = t["ones_col_bf"][:]
    oner_f = t["ones_row_f32"][:]

    barrier = [(s, k.v(s)) for s in ("qsp", "qpool", "dve", "act")]

    u_done = []
    ubank = {}
    for p in ("q", "k", "v"):
        BT = t[f"s_B{p}T"]
        utgt = t[f"s_u{p}"]
        for c in range(4):
            def mms(e, c=c, BT=BT):
                r = None
                for d in range(ND):
                    r = e.matmul(PB[c][0:16, :], BT[:, d * RANK:(d + 1) * RANK],
                                 XN[:, d * T + c * 512: d * T + (c + 1) * 512],
                                 start=(d == 0), stop=(d == ND - 1))
                return r
            w = list(barrier) if (p == "q" and c == 0) else []
            if c in ubank:
                w = w + [ubank[c]]
            tk = k.emit("pe", mms, waits=w, inc="pe")
            ev = k.emit("act", lambda e, c=c, utgt=utgt:
                        e.mul(utgt[:, c * 512:(c + 1) * 512], PB[c][0:16, :], SCALING),
                        waits=[tk], inc="act")
            ubank[c] = ev
            u_done.append(ev)
    pb_free = {i: list(u_done) for i in range(8)}

    qh_w, kh_w = PT[:, 0:T], PT[:, T:2 * T]
    sq_w = PT[:, 3 * T:4 * T]
    EB = [PT[:, 2 * T:2 * T + 1024], PT[:, 2 * T + 1024:2 * T + 2048]]
    RREP = PTf[:, 3 * T // 2: 3 * T // 2 + 1024]
    RROW = [PTf[0:1, 3 * T // 2 + c * 512: 3 * T // 2 + (c + 1) * 512] for c in range(2)]
    VW = [SM[:, d * 512:(d + 1) * 512] for d in range(ND)]
    VBUF = SM[:, 8192:16384]
    QW = [STR[:, d * 128:(d + 1) * 128] for d in range(ND)]
    KW = [STR[:, 2048 + d * 128:(d + 1) * 128 + 2048] for d in range(ND)]
    AQ, AK = STR[0:16, 4096:4224], STR[0:16, 4224:4352]
    AV = STR[0:16, 4352:4864]
    STRv = STR.ap().rearrange("p (a z) -> p a z", a=40)
    SMv = SM.ap().rearrange("p (a z) -> p a z", a=32)

    qbar = [("qsp", k.v("qsp"))]
    vbuf_free = []
    vw_free = list(qbar)
    qkw_free = list(qbar)
    av_free = list(qbar)
    eb_free = [[], []]
    pss_free = [[], []]
    sqw_free = []
    oh_last = []

    for g in groups:
        vw_tok = k.dma("pool", SMv[:, 0:16, :],
                       t["l0_wvT"].rearrange("(a p) o -> p a o", p=128)
                       [:, :, g * 512:(g + 1) * 512], waits=vw_free)
        av_tok = k.dma("pool", AV, t["l0_AvT"][:, g * 512:(g + 1) * 512],
                       waits=av_free)
        v_evacs = []
        vlast = None
        for tt in range(NT):
            pb = PB[tt % 2]
            w = [vw_tok, av_tok] + pb_free.get(tt % 2, []) + vbuf_free

            def vmm(e, tt=tt, pb=pb, g=g):
                for d in range(ND):
                    e.matmul(pb[:], XN[:, d * T + tt * 128: d * T + (tt + 1) * 128],
                             VW[d], start=(d == 0), stop=False)
                e.matmul(pb[:], t["s_uv"][:, tt * 128:(tt + 1) * 128], AV,
                         start=False, stop=False)
                return e.matmul(pb[:], t["ones_row_bf"][:],
                                t["s_l0_bv_row"][:, g * 512:(g + 1) * 512],
                                start=False, stop=True)
            tk = k.emit("pe", vmm, waits=w, inc="pe")
            vlast = tk
            ev = k.emit("act", lambda e, tt=tt, pb=pb:
                        e.copy(VBUF[:, tt * 512:(tt + 1) * 512], pb[:]),
                        waits=[tk], inc="act")
            pb_free[tt % 2] = [ev]
            v_evacs.append(ev)
        vbuf_free = []
        vw_free = [vlast]
        for hh in heads:
            h = g * 4 + hh
            qw_tok = k.dma("pool", STRv[:, 0:16, :],
                           t["l0_wqT"].rearrange("(a p) o -> p a o", p=128)
                           [:, :, h * 128:(h + 1) * 128], waits=qkw_free)
            kw_tok = k.dma("pool", STRv[:, 16:32, :],
                           t["l0_wkT"].rearrange("(a p) o -> p a o", p=128)
                           [:, :, h * 128:(h + 1) * 128], waits=[])
            aq_tok = k.dma("pool", AQ, t["l0_AqT"][:, h * 128:(h + 1) * 128])
            ak_tok = k.dma("pool", AK, t["l0_AkT"][:, h * 128:(h + 1) * 128])
            proj_last = []
            evs = {"q": [], "k": []}
            for nm, wtiles, atile, u, bias, dst, wtok in (
                    ("q", QW, AQ, t["s_uq"], t["s_l0_bq"], qh_w, [qw_tok, aq_tok]),
                    ("k", KW, AK, t["s_uk"], t["s_l0_bk"], kh_w, [kw_tok, ak_tok])):
                for c in range(4):
                    pb = PB[c]

                    def qmm(e, c=c, pb=pb, wtiles=wtiles, atile=atile, u=u):
                        for d in range(ND):
                            e.matmul(pb[:], wtiles[d],
                                     XN[:, d * T + c * 512: d * T + (c + 1) * 512],
                                     start=(d == 0), stop=False)
                        return e.matmul(pb[:], atile, u[:, c * 512:(c + 1) * 512],
                                        start=False, stop=True)
                    tk = k.emit("pe", qmm,
                                waits=wtok + pb_free.get(c, []) + pss_free[c // 2],
                                inc="pe")
                    proj_last.append(tk)
                    ev = k.emit("act", lambda e, c=c, pb=pb, bias=bias, dst=dst, h=h:
                                e.activation(dst[:, c * 512:(c + 1) * 512], pb[:],
                                             AF.Identity, bias=bias[:, h:h + 1],
                                             scale=1.0),
                                waits=[tk], inc="act")
                    pb_free[c] = [ev]
                    evs[nm].append(ev)
            qkw_free = proj_last
            pss_free = [[], []]
            sq_tok = k.emit("act", lambda e: e.activation(sq_w, qh_w, AF.Square),
                            waits=evs["q"] + sqw_free, inc="act")
            red1 = []
            for c in range(4):
                tk = k.emit("pe", lambda e, c=c: e.matmul(
                    PB[4 + c][0:1, :], onec, sq_w[:, c * 512:(c + 1) * 512],
                    start=True, stop=True),
                    waits=[sq_tok] + pb_free.get(4 + c, []), inc="pe")
                rd = k.emit("dve", lambda e, c=c: e.reduce_max(
                    ST[0:1, c:c + 1], PB[4 + c][0:1, :], axis=AX.X),
                    waits=[tk], inc="dve")
                red1.append(rd)
            sqk_tok = k.emit("act", lambda e: e.activation(sq_w, kh_w, AF.Square),
                             waits=evs["k"] + red1, inc="act")
            red2 = []
            for c in range(4):
                tk = k.emit("pe", lambda e, c=c: e.matmul(
                    PB[4 + c][0:1, :], onec, sq_w[:, c * 512:(c + 1) * 512],
                    start=True, stop=True), waits=[sqk_tok] + red1, inc="pe")
                rd = k.emit("dve", lambda e, c=c: e.reduce_max(
                    ST[0:1, 8 + c:9 + c], PB[4 + c][0:1, :], axis=AX.X),
                    waits=[tk], inc="dve")
                red2.append(rd)
            bnda = k.emit("dve", lambda e: e.reduce_max(
                ST[0:1, 4:5], ST[0:1, 0:4], axis=AX.X), waits=red2, inc="dve")
            bndb = k.emit("dve", lambda e: e.reduce_max(
                ST[0:1, 12:13], ST[0:1, 8:12], axis=AX.X), waits=[bnda],
                inc="dve")
            bnd = k.emit("dve", lambda e: e.tensor_tensor(
                ST[0:1, 13:14], ST[0:1, 4:5], ST[0:1, 12:13], OP.mult),
                waits=[bndb], inc="dve")
            bnd2 = k.emit("act", lambda e: e.sqrt(ST[0:1, 14:15], ST[0:1, 13:14]),
                          waits=[bnd], inc="act")
            bnd3 = k.emit("dve", lambda e: e.tensor_scalar(
                ST[0:1, 15:16], ST[0:1, 14:15], -SCALE, None, OP.mult, OP.bypass),
                waits=[bnd2], inc="dve")
            rep = k.emit("pe", lambda e: e.matmul(
                PB[4][:, 0:1], oner_f, ST[0:1, 15:16],
                start=True, stop=True), waits=[bnd3] + pb_free.get(4, []), inc="pe")
            ebias_tok = k.emit("act", lambda e: e.copy(ST[:, 16:17], PB[4][:, 0:1]),
                               waits=[rep], inc="act")
            EBIAS = ST[:, 16:17]
            sqw_free = []
            for qh in range(2):
                exp_toks = {}
                pv_toks = {}
                for kt in range(NT):
                    par = kt % 2
                    pss = (PB[0], PB[1]) if par == 0 else (PB[2], PB[3])

                    def smm(e, kt=kt, pss=pss, qh=qh):
                        r = None
                        for c in range(2):
                            r = e.matmul(
                                pss[c][:], kh_w[:, kt * 128:(kt + 1) * 128],
                                qh_w[:, qh * 1024 + c * 512: qh * 1024 + (c + 1) * 512],
                                start=True, stop=True)
                        return r
                    w = [ebias_tok] + (
                        [exp_toks[kt - 2]] if kt >= 2 else list(pss_free[par]))
                    st_tok = k.emit("pe", smm, waits=w, inc="pe")

                    def ex(e, par=par, pss=pss):
                        r = None
                        for c in range(2):
                            r = e.activation(EB[par][:, c * 512:(c + 1) * 512],
                                             pss[c][:], AF.Exp, bias=EBIAS,
                                             scale=SCALE)
                        return r
                    wexp = [st_tok] + (
                        [pv_toks[kt - 2]] if kt >= 2 else list(eb_free[par]))
                    exp_toks[kt] = k.emit("act", ex, waits=wexp, inc="act")

                    wpv = [exp_toks[kt]]
                    if kt == 0:
                        for i4 in (4, 5, 6, 7):
                            wpv += pb_free.get(i4, [])
                        wpv += v_evacs

                    def pv(e, kt=kt, par=par, hh=hh):
                        for c in range(2):
                            e.matmul(PB[6 + c][0:1, :], onec,
                                     EB[par][:, c * 512:(c + 1) * 512],
                                     start=(kt == 0), stop=(kt == NT - 1))
                        r = None
                        for c in range(2):
                            r = e.matmul(
                                PB[4 + c][:],
                                VBUF[:, kt * 512 + hh * 128: kt * 512 + (hh + 1) * 128],
                                EB[par][:, c * 512:(c + 1) * 512],
                                start=(kt == 0), stop=(kt == NT - 1))
                        return r
                    pv_toks[kt] = k.emit("pe", pv, waits=wpv, inc="pe")
                    eb_free[par] = [pv_toks[kt]]
                    pss_free[par] = [exp_toks[kt]]
                rr = []
                for c in range(2):
                    rr.append(k.emit("dve", lambda e, c=c: e.reciprocal(
                        RROW[c], PB[6 + c][0:1, :]),
                        waits=[pv_toks[NT - 1]] + sqw_free, inc="dve"))
                reps = []
                for c in range(2):
                    tk = k.emit("pe", lambda e, c=c: e.matmul(
                        PB[c][:], oner_f, RROW[c], start=True, stop=True),
                        waits=[rr[c]] + pss_free[0], inc="pe")
                    reps.append(tk)
                cp = k.emit("dve", lambda e: (
                    e.tensor_copy(RREP[:, 0:512], PB[0][:]),
                    e.tensor_copy(RREP[:, 512:1024], PB[1][:]))[-1],
                    waits=reps, inc="dve")
                ot = k.emit("dve", lambda e, h=h, qh=qh: (
                    e.tensor_tensor(BIG[:, h * T + qh * 1024: h * T + qh * 1024 + 512],
                                    PB[4][:], RREP[:, 0:512], OP.mult),
                    e.tensor_tensor(BIG[:, h * T + qh * 1024 + 512: h * T + (qh + 1) * 1024],
                                    PB[5][:], RREP[:, 512:1024], OP.mult))[-1],
                    waits=[cp], inc="dve")
                oh_last = [ot]
                sqw_free = [ot]
                pb_free[0] = [cp]
                pb_free[1] = [cp]
                pb_free[4] = [ot]
                pb_free[5] = [ot]
                pb_free[6] = [rr[0]]
                pb_free[7] = [rr[1]]
                pss_free[0] = [cp]
                pss_free[1] = [exp_toks[NT - 1]]
            vbuf_free = [pv_toks[NT - 1]]
    return oh_last, pb_free


def emit_phaseWo(k, t, oh_last, pb_free):
    BIG, PT, STR = t["BIG"], t["POOLT"], t["STREAM"]
    PB = [t[f"PB{i}"] for i in range(8)]
    uo = t["s_uq"]
    BT = t["s_BoT"]
    STRv = STR.ap().rearrange("p (a z) -> p a z", a=40)
    AO = STR[0:16, 4096:4224]
    S0T = PT[:, 0:T]

    u_done = []
    for c in range(4):
        def mms(e, c=c):
            r = None
            for d in range(ND):
                r = e.matmul(PB[c][0:16, :], BT[:, d * RANK:(d + 1) * RANK],
                             BIG[:, d * T + c * 512: d * T + (c + 1) * 512],
                             start=(d == 0), stop=(d == ND - 1))
            return r
        tk = k.emit("pe", mms, waits=oh_last + pb_free.get(c, []), inc="pe")
        ev = k.emit("act", lambda e, c=c:
                    e.mul(uo[:, c * 512:(c + 1) * 512], PB[c][0:16, :], SCALING),
                    waits=[tk], inc="act")
        u_done.append(ev)
    pbf = {c: list(u_done) for c in range(8)}
    w_free = {0: list(oh_last), 1: list(oh_last)}
    ao_free = list(oh_last)
    s0_free = []
    for o in range(ND):
        buf = o % 2
        wtok = k.dma("pool", STRv[:, buf * 16:(buf + 1) * 16, :],
                     t["l0_woT"].rearrange("(a p) o -> p a o", p=128)
                     [:, :, o * 128:(o + 1) * 128], waits=w_free[buf])
        ao_tok = k.dma("pool", AO, t["l0_AoT"][:, o * 128:(o + 1) * 128],
                       waits=ao_free)
        mm_toks = []
        evs = []
        for c in range(4):
            pb = PB[c]

            def mm(e, c=c, pb=pb, buf=buf):
                for d in range(ND):
                    e.matmul(pb[:],
                             STR[:, buf * 2048 + d * 128: buf * 2048 + (d + 1) * 128],
                             BIG[:, d * T + c * 512: d * T + (c + 1) * 512],
                             start=(d == 0), stop=False)
                return e.matmul(pb[:], AO, uo[:, c * 512:(c + 1) * 512],
                                start=False, stop=True)
            tk = k.emit("pe", mm, waits=[wtok, ao_tok] + pbf.get(c, []), inc="pe")
            mm_toks.append(tk)
            ev = k.emit("act", lambda e, c=c, pb=pb, o=o:
                        e.activation(S0T[:, c * 512:(c + 1) * 512], pb[:],
                                     AF.Identity, bias=t["s_l0_bo"][:, o:o + 1],
                                     scale=1.0), waits=[tk] + s0_free, inc="act")
            pbf[c] = [ev]
            evs.append(ev)
        w_free[buf] = mm_toks
        ao_free = mm_toks
        st = k.dma("sp", t["sc_s0"][:, o, :], S0T, waits=evs)
        s0_free = [st]
    return


def emit_phaseC(k, t, after):
    XN, BIG, SM, PT, STR = (t[n] for n in ("XN", "BIG", "SMALL", "POOLT", "STREAM"))
    PB = [t[f"PB{i}"] for i in range(8)]
    BIGf = BIG.ap().bitcast(F32)
    PTf = PT.ap().bitcast(F32)
    ST = t["s_stat"].ap()
    onec_f = t["ones_col_f32"][:]
    oner_f = t["ones_row_f32"][:]

    IN = [BIGf[:, i * 4096:(i + 1) * 4096] for i in range(2)]
    in_free = [list(after), list(after)]
    cell_base = {"w1": 0, "w2": 32}

    def chunks_of(width, na):
        out = []
        if width >= 4096:
            nh = width // 4096
            for a in range(na):
                for hh in range(nh):
                    out.append((a, a + 1, hh * 4096, (hh + 1) * 4096))
        else:
            per = 4096 // width
            for a0 in range(0, na, per):
                out.append((a0, a0 + per, 0, width))
        return out

    rd_toks = []
    for wi, (nm, width, na) in enumerate((("w1", DFF, ND), ("w2", D, NF))):
        src_v = t[f"l0_{nm}T"].rearrange("(a p) f -> p a f", p=128)
        for ch, (a0, a1, f0, f1) in enumerate(chunks_of(width, na)):
            b = ch % 2
            ld = k.dma("sp", IN[b].rearrange("p (a f) -> p a f", a=a1 - a0),
                       src_v[:, a0:a1, f0:f1], waits=in_free[b])
            cell = cell_base[nm] + ch
            rd = k.emit("dve", lambda e, b=b, cell=cell: e.tensor_reduce(
                ST[:, cell:cell + 1], IN[b], axis=AX.X, op=OP.add,
                apply_absolute_value=True), waits=[ld] + after, inc="dve")
            rd_toks.append(rd)
            in_free[b] = [rd]
    hop = k.emit("act", lambda e: e.nop(), waits=[rd_toks[-1]], inc="act")
    acc_toks = {}
    for j, nm in enumerate(("w1", "w2")):
        a_ = k.emit("dve", lambda e, j=j: e.reduce_sum(
            ST[:, 64 + j:65 + j], ST[:, 32 * j:32 * j + 32], axis=AX.X),
            waits=[hop], inc="dve")
        acc_toks[nm] = a_
    hop2 = k.emit("act", lambda e: e.nop(),
                  waits=[acc_toks["w1"], acc_toks["w2"]], inc="act")
    thr_toks = []
    for j, nm in enumerate(("w1", "w2")):
        mm = k.emit("pe", lambda e, j=j: e.matmul(
            PB[0][0:1, 0:1], ST[:, 64 + j:65 + j], onec_f,
            start=True, stop=True), waits=[hop2] + after, inc="pe")
        sc = k.emit("dve", lambda e: e.tensor_scalar(
            ST[0:1, 68:69], PB[0][0:1, 0:1], 0.7 / (D * DFF), None,
            OP.mult, OP.bypass), waits=[mm], inc="dve")
        rep = k.emit("pe", lambda e: e.matmul(
            PB[1][:, 0:1], oner_f, ST[0:1, 68:69], start=True, stop=True),
            waits=[sc], inc="pe")
        cp = k.emit("dve", lambda e, j=j: e.tensor_copy(
            ST[:, 66 + j:67 + j], PB[1][:, 0:1]), waits=[rep], inc="dve")
        thr_toks.append(cp)

    M1 = BIG[:, 16384:20480]
    M2 = BIG[:, 20480:24576]
    WQ = BIG[:, 24576:28672]
    m_free = list(thr_toks)
    wq_free = []
    for j, (nm, width, na, dst) in enumerate(
            (("w1", DFF, ND, "sc_w1q"), ("w2", D, NF, "sc_w2q"))):
        src_v = t[f"l0_{nm}T"].rearrange("(a p) f -> p a f", p=128)
        dst_v = t[dst].rearrange("a p f -> p a f")
        thr = ST[:, 66 + j:67 + j]
        for ch, (a0, a1, f0, f1) in enumerate(chunks_of(width, na)):
            b = ch % 2
            ld = k.dma("sp", IN[b].rearrange("p (a f) -> p a f", a=a1 - a0),
                       src_v[:, a0:a1, f0:f1], waits=in_free[b])
            m1t = k.emit("dve", lambda e, b=b, thr=thr: e.tensor_scalar(
                M1, IN[b], thr, None, OP.is_ge, OP.bypass),
                waits=[ld] + m_free, inc="dve")
            m2t = k.emit("dve", lambda e, b=b, thr=thr: e.tensor_scalar(
                M2, IN[b], -1.0, thr, OP.mult, OP.is_ge),
                waits=[ld] + m_free, inc="dve")
            wq = k.emit("dve", lambda e: e.tensor_tensor(WQ, M1, M2, OP.subtract),
                        waits=[m1t, m2t] + wq_free, inc="dve")
            in_free[b] = [m1t, m2t]
            m_free = [wq]
            st = k.dma("sp", dst_v[:, a0:a1, f0:f1],
                       WQ.rearrange("p (a f) -> p a f", a=a1 - a0), waits=[wq])
            wq_free = [st]

    W1S = [STR[:, i * 2048:(i + 1) * 2048] for i in range(2)]
    W2S = [SM[:, i * 8192:(i + 1) * 8192] for i in range(2)]
    G0 = [PT[:, i * 512:(i + 1) * 512] for i in range(2)]
    S0L = [PT[:, 1024 + i * 512: 1024 + (i + 1) * 512] for i in range(2)]
    X0L = [PT[:, 2048 + i * 512: 2048 + (i + 1) * 512] for i in range(2)]
    T1 = [PT[:, 3072 + i * 512: 3072 + (i + 1) * 512] for i in range(2)]
    w1s_free = [list(wq_free), list(wq_free)]
    w2s_free = [list(after), list(after)]
    h_free = {f: list(m_free) + list(in_free[0]) + list(in_free[1])
              for f in range(NF)}
    g0_free = [list(after), list(after)]
    s0l_free = [list(after), list(after)]
    x0l_free = [list(after), list(after)]
    pb_free = {i: [] for i in range(8)}
    sc_w1q_v = t["sc_w1q"].rearrange("a p f -> p a f")
    sc_w2q_v = t["sc_w2q"].rearrange("a p f -> p a f")
    for tc in range(4):
        tsl = slice(tc * 512, (tc + 1) * 512)
        for f in range(NF):
            b = f % 2
            ld = k.dma("sp", W1S[b].rearrange("p (a z) -> p a z", a=16),
                       sc_w1q_v[:, :, f * 128:(f + 1) * 128], waits=w1s_free[b])
            pb = PB[f % 4]

            def mm(e, b=b, pb=pb, tc=tc):
                r = None
                for d in range(ND):
                    r = e.matmul(pb[:], W1S[b][:, d * 128:(d + 1) * 128],
                                 XN[:, d * T + tc * 512: d * T + (tc + 1) * 512],
                                 start=(d == 0), stop=(d == ND - 1))
                return r
            tk = k.emit("pe", mm, waits=[ld] + pb_free.get(f % 4, []), inc="pe")
            w1s_free[b] = [tk]
            ev = k.emit("act", lambda e, f=f, pb=pb: e.activation(
                BIG[:, f * 512:(f + 1) * 512], pb[:], AF.Gelu,
                bias=t["s_l0_b1"][:, f:f + 1], scale=1.0),
                waits=[tk] + h_free.get(f, []), inc="act")
            pb_free[f % 4] = [ev]
        for o in range(ND):
            b = o % 2
            ld = k.dma("sp", W2S[b].rearrange("p (a z) -> p a z", a=64),
                       sc_w2q_v[:, :, o * 128:(o + 1) * 128], waits=w2s_free[b])
            lds = k.dma("sp", S0L[b], t["sc_s0"][:, o, tsl], waits=s0l_free[b])
            ldx = k.dma("pool", X0L[b], t["sc_x0d"][:, o, tsl], waits=x0l_free[b])
            pb = PB[4 + o % 4]

            def mm2(e, b=b, pb=pb):
                r = None
                for f in range(NF):
                    r = e.matmul(pb[:], W2S[b][:, f * 128:(f + 1) * 128],
                                 BIG[:, f * 512:(f + 1) * 512],
                                 start=(f == 0), stop=(f == NF - 1))
                return r
            hw = [ld] + pb_free.get(4 + o % 4, [])
            if o == 0:
                for cc in range(4):
                    hw += pb_free.get(cc, [])
            tk = k.emit("pe", mm2, waits=hw, inc="pe")
            w2s_free[b] = [tk]
            ev = k.emit("act", lambda e, b=b, pb=pb, o=o: e.activation(
                G0[b], pb[:], AF.Silu, bias=t["s_l0_b2"][:, o:o + 1], scale=1.0),
                waits=[tk] + g0_free[b], inc="act")
            pb_free[4 + o % 4] = [ev]
            t1 = k.emit("dve", lambda e, b=b: e.tensor_tensor(
                T1[b], S0L[b], G0[b], OP.mult), waits=[ev, lds], inc="dve")
            g0_free[b] = [t1]
            s0l_free[b] = [t1]
            x1 = k.emit("dve", lambda e, b=b: e.tensor_tensor(
                T1[b], T1[b], X0L[b], OP.add), waits=[t1, ldx], inc="dve")
            x0l_free[b] = [x1]
            stx = k.dma("sp", t["sc_x1"][:, o, tsl], T1[b], waits=[x1])
            g0_free[b].append(stx)
        last_mm = w2s_free[0] + w2s_free[1]
        for f in range(NF):
            h_free[f] = last_mm
    return [x1, stx]


def emit_phaseD(k, t, after):
    XN, PT, STR = t["XN"], t["POOLT"], t["STREAM"]
    PB = [t[f"PB{i}"] for i in range(8)]
    PTf = PT.ap().bitcast(F32)
    onec = t["ones_col_bf"][:]
    oner_f = t["ones_row_f32"][:]
    X1L = [STR[:, i * 2048:(i + 1) * 2048] for i in range(2)]
    STRf = STR.ap().bitcast(F32)
    MSQ = STRf[0:1, 2048:2560]
    x1_free = [after, after]
    sq_free = list(after)
    pb_rows_free = list(after)
    mm_toks = []
    for d in range(ND):
        b = d % 2
        ld = k.dma("sp", X1L[b], t["sc_x1"][:, d, :], waits=x1_free[b])
        sq = k.emit("act", lambda e, b=b: e.activation(
            PT[:, 0:T], X1L[b], AF.Square), waits=[ld] + sq_free, inc="act")

        def mm(e, b=b, d=d):
            r = None
            for c in range(4):
                e.matmul(PB[c][0:1, :], onec,
                         X1L[b][:, c * 512:(c + 1) * 512],
                         start=(d == 0), stop=(d == ND - 1))
                r = e.matmul(PB[4 + c][0:1, :], onec,
                             PT[:, c * 512:(c + 1) * 512],
                             start=(d == 0), stop=(d == ND - 1))
            return r
        tk = k.emit("pe", mm, waits=[ld, sq] + pb_rows_free, inc="pe")
        pb_rows_free = []
        mm_toks.append(tk)
        x1_free[b] = [tk]
        sq_free = [tk]
    MU = PTf[0:1, 0:T]
    RS = PTf[0:1, T:2 * T]
    fins = []
    for c in range(4):
        d1 = k.emit("dve", lambda e, c=c: e.tensor_scalar(
            MU[:, c * 512:(c + 1) * 512], PB[c][0:1, :], 1.0 / D, None,
            OP.mult, OP.bypass), waits=[mm_toks[-1]], inc="dve")
        d2 = k.emit("dve", lambda e, c=c: e.tensor_scalar(
            RS[:, c * 512:(c + 1) * 512], PB[4 + c][0:1, :], 1.0 / D, None,
            OP.mult, OP.bypass), waits=[d1], inc="dve")
        d3 = k.emit("dve", lambda e, c=c: e.tensor_tensor(
            MSQ, MU[:, c * 512:(c + 1) * 512], MU[:, c * 512:(c + 1) * 512],
            OP.mult), waits=[d2], inc="dve")
        d4 = k.emit("dve", lambda e, c=c: e.tensor_tensor(
            RS[:, c * 512:(c + 1) * 512], RS[:, c * 512:(c + 1) * 512], MSQ,
            OP.subtract), waits=[d3], inc="dve")
        fin = k.emit("dve", lambda e, c=c: e.tensor_scalar(
            RS[:, c * 512:(c + 1) * 512], RS[:, c * 512:(c + 1) * 512],
            LN_EPS, None, OP.add, OP.bypass), waits=[d4], inc="dve")
        sq = k.emit("act", lambda e, c=c: e.sqrt(
            RS[:, c * 512:(c + 1) * 512], RS[:, c * 512:(c + 1) * 512]),
            waits=[fin], inc="act")
        rc = k.emit("dve", lambda e, c=c: e.reciprocal(
            RS[:, c * 512:(c + 1) * 512], RS[:, c * 512:(c + 1) * 512]),
            waits=[sq], inc="dve")
        fins.append(rc)
    reps = []
    for c in range(4):
        tk = k.emit("pe", lambda e, c=c: (
            e.matmul(PB[c][:], oner_f, MU[:, c * 512:(c + 1) * 512],
                     start=True, stop=True),
            e.matmul(PB[4 + c][:], oner_f, RS[:, c * 512:(c + 1) * 512],
                     start=True, stop=True))[-1], waits=fins, inc="pe")
        reps.append(tk)
    TMP = PTf[:, 1024:1536]
    tmp_free = []
    xn_toks = []
    for d in range(ND):
        b = d % 2
        ld = k.dma("sp", X1L[b], t["sc_x1"][:, d, :], waits=x1_free[b])
        for c in range(4):
            t1 = k.emit("dve", lambda e, b=b, c=c: e.tensor_tensor(
                TMP, X1L[b][:, c * 512:(c + 1) * 512], PB[c][:], OP.subtract),
                waits=[ld] + reps + tmp_free, inc="dve")
            t2 = k.emit("dve", lambda e, b=b, c=c, d=d: e.tensor_tensor(
                XN[:, d * T + c * 512: d * T + (c + 1) * 512], TMP, PB[4 + c][:],
                OP.mult), waits=[t1], inc="dve")
            tmp_free = [t2]
            xn_toks.append(t2)
        x1_free[b] = [tmp_free[0]]
    return xn_toks[-4:]



def emit_phaseE(k, t, after):
    XN, BIG, SM, PT, STR = (t[n] for n in ("XN", "BIG", "SMALL", "POOLT", "STREAM"))
    PB = [t[f"PB{i}"] for i in range(8)]
    PTf = PT.ap().bitcast(F32)
    SMf = SM.ap().bitcast(F32)
    onec = t["ones_col_bf"][:]
    oner_f = t["ones_row_f32"][:]
    STRv = STR.ap().rearrange("p (a z) -> p a z", a=40)
    CW = t["s_conv_w"].ap()

    ACC = [PTf[:, i * 2048:(i + 1) * 2048] for i in range(2)]
    acc_free = [list(after), list(after)]
    big_free = list(after)
    conv_toks = []
    for d in range(ND):
        b = d % 2

        tk = k.emit("dve", lambda e, d=d, b=b: e.tensor_scalar(
            ACC[b], XN[:, d * T:(d + 1) * T], CW[:, d * 4 + 3:d * 4 + 4],
            None, OP.mult, OP.bypass), waits=acc_free[b] + after, inc="dve")
        for j in (2, 1, 0):
            off = 3 - j
            tk = k.emit("dve", lambda e, d=d, b=b, j=j, off=off:
                        e.scalar_tensor_tensor(
                            ACC[b][:, off:T], XN[:, d * T: d * T + T - off],
                            CW[:, d * 4 + j:d * 4 + j + 1], ACC[b][:, off:T],
                            OP.mult, OP.add), waits=[tk], inc="dve")
        ev = k.emit("act", lambda e, d=d, b=b: e.activation(
            BIG[:, d * T:(d + 1) * T], ACC[b], AF.Silu,
            bias=t["s_l1_conv_b"][:, d:d + 1], scale=1.0),
            waits=[tk] + big_free, inc="act")
        acc_free[b] = [ev]
        conv_toks.append(ev)
    big_free = []

    GT = PT[:, 0:T]
    w_free = {0: list(after), 1: list(after)}
    gt_free = []
    pbf = {c: list(after) for c in range(8)}
    gate_last = []
    for o in range(ND):
        b = o % 2
        wtok = k.dma("pool", STRv[:, b * 16:(b + 1) * 16, :],
                     t["l1_wgT"].rearrange("(a p) o2 -> p a o2", p=128)
                     [:, :, o * 128:(o + 1) * 128], waits=w_free[b])
        evs = []
        for c in range(4):
            def mmg(e, c=c, b=b):
                r = None
                for d in range(ND):
                    r = e.matmul(PB[c][:],
                                 STR[:, b * 2048 + d * 128: b * 2048 + (d + 1) * 128],
                                 BIG[:, d * T + c * 512: d * T + (c + 1) * 512],
                                 start=(d == 0), stop=(d == ND - 1))
                return r
            tk = k.emit("pe", mmg, waits=[wtok] + pbf.get(c, []) + conv_toks[-1:],
                        inc="pe")
            ev = k.emit("act", lambda e, c=c, o=o: e.activation(
                GT[:, c * 512:(c + 1) * 512], PB[c][:], AF.Sigmoid,
                bias=t["s_l1_gate_b"][:, o:o + 1], scale=1.0),
                waits=[tk] + gt_free, inc="act")
            pbf[c] = [ev]
            evs.append(ev)
        w_free[b] = [tk]
        st = k.dma("sp", t["sc_gate"][:, o, :], GT, waits=evs)
        gt_free = [st]
        gate_last = [tk, st]

    KT = PT[:, 0:T]
    QT = PT[:, 2 * T:3 * T]
    SQS = PT[:, T:T + 512]
    SCT = SM[:, 0:T]
    GL = SM[:, T:2 * T]
    ROW = SMf[0:1, 4096:4608]
    kt_free = list(gate_last) + gt_free
    qt_free = list(after)
    sqs_free = list(after)
    sct_free = list(after)
    gl_free = list(after)

    ROW2 = SMf[0:1, 4608:5120]

    def l2norm_apply(dst, o, pbf_local):
        toks = []
        sq_prev = sqs_free[:]
        rc_prev = []
        mm_prev = []
        for c in range(4):
            sq = k.emit("act", lambda e, c=c, dst=dst: e.activation(
                SQS, dst[:, c * 512:(c + 1) * 512], AF.Square),
                waits=pbf_local + sq_prev, inc="act")
            mm = k.emit("pe", lambda e, c=c: e.matmul(
                PB[4 + c][0:1, :], onec, SQS, start=True, stop=True),
                waits=[sq] + mm_prev, inc="pe")
            sq_prev = [mm]
            sqr = k.emit("act", lambda e, c=c: e.sqrt(ROW, PB[4 + c][0:1, :]),
                         waits=[mm] + rc_prev, inc="act")
            mx = k.emit("dve", lambda e: e.tensor_scalar(
                ROW, ROW, NORM_EPS, None, OP.max, OP.bypass),
                waits=[sqr], inc="dve")
            rc = k.emit("dve", lambda e: e.reciprocal(ROW2, ROW),
                        waits=[mx], inc="dve")
            rc_prev = [rc]
            rep = k.emit("pe", lambda e, c=c: e.matmul(
                PB[4 + c][:], oner_f, ROW2, start=True, stop=True),
                waits=[rc], inc="pe")
            mm_prev = [rep]
            sc = k.emit("dve", lambda e, c=c, dst=dst: e.tensor_tensor(
                dst[:, c * 512:(c + 1) * 512], dst[:, c * 512:(c + 1) * 512],
                PB[4 + c][:], OP.mult), waits=[rep], inc="dve")
            toks.append(sc)
        return toks

    big_free2 = {}
    scan_toks = {}
    for o in range(ND):
        b = o % 2
        wtok = k.dma("pool", STRv[:, b * 16:(b + 1) * 16, :],
                     t["l1_wkT"].rearrange("(a p) o2 -> p a o2", p=128)
                     [:, :, o * 128:(o + 1) * 128], waits=w_free[b])
        kevs = []
        ktoks = []
        for c in range(4):
            def mmk(e, c=c, b=b):
                r = None
                for d in range(ND):
                    r = e.matmul(PB[c][:],
                                 STR[:, b * 2048 + d * 128: b * 2048 + (d + 1) * 128],
                                 XN[:, d * T + c * 512: d * T + (c + 1) * 512],
                                 start=(d == 0), stop=(d == ND - 1))
                return r
            tk = k.emit("pe", mmk, waits=[wtok] + pbf.get(c, []), inc="pe")
            ktoks.append(tk)
            ev = k.emit("act", lambda e, c=c: e.copy(
                KT[:, c * 512:(c + 1) * 512], PB[c][:]),
                waits=[tk] + kt_free, inc="act")
            pbf[c] = [ev]
            kevs.append(ev)
        kn = l2norm_apply(KT, o, kevs)
        sqs_free = kn[-1:]
        wtok2 = k.dma("pool", STRv[:, b * 16:(b + 1) * 16, :],
                      t["l1_wvT"].rearrange("(a p) o2 -> p a o2", p=128)
                      [:, :, o * 128:(o + 1) * 128], waits=ktoks)
        kv_toks = []
        for c in range(4):
            def mmv(e, c=c, b=b):
                r = None
                for d in range(ND):
                    r = e.matmul(PB[c][:],
                                 STR[:, b * 2048 + d * 128: b * 2048 + (d + 1) * 128],
                                 XN[:, d * T + c * 512: d * T + (c + 1) * 512],
                                 start=(d == 0), stop=(d == ND - 1))
                return r
            tk = k.emit("pe", mmv, waits=[wtok2] + pbf.get(c, []) + kn[-1:],
                        inc="pe")
            kv = k.emit("dve", lambda e, c=c, o=o: e.tensor_tensor(
                BIG[:, o * T + c * 512: o * T + (c + 1) * 512],
                KT[:, c * 512:(c + 1) * 512], PB[c][:], OP.mult),
                waits=[tk] + big_free2.get(o, []) + kn, inc="dve")
            pbf[c] = [kv]
            kv_toks.append(kv)
        w_free[b] = [tk]
        kt_free = kv_toks
        gld = k.dma("sp", GL, t["sc_gate"][:, o, :], waits=gl_free)
        g_ap = t["s_l1_gamma_cols"][:, o:o + 1].broadcast_to((128, T))
        sc_t = k.emit("dve", lambda e, o=o, g_ap=g_ap: e.tensor_tensor_scan(
            SCT, g_ap, BIG[:, o * T:(o + 1) * T], 0.0, OP.mult, OP.add),
            waits=kv_toks + sct_free, inc="dve")
        mem = k.emit("dve", lambda e, o=o: e.tensor_tensor(
            BIG[:, o * T:(o + 1) * T], SCT, GL, OP.mult),
            waits=[sc_t, gld], inc="dve")
        sct_free = [mem]
        gl_free = [mem]
        scan_toks[o] = mem
        wtok3 = k.dma("pool", STRv[:, b * 16:(b + 1) * 16, :],
                      t["l1_wqT"].rearrange("(a p) o2 -> p a o2", p=128)
                      [:, :, o * 128:(o + 1) * 128], waits=[tk])
        qevs = []
        for c in range(4):
            def mmq(e, c=c, b=b):
                r = None
                for d in range(ND):
                    r = e.matmul(PB[c][:],
                                 STR[:, b * 2048 + d * 128: b * 2048 + (d + 1) * 128],
                                 XN[:, d * T + c * 512: d * T + (c + 1) * 512],
                                 start=(d == 0), stop=(d == ND - 1))
                return r
            tk2 = k.emit("pe", mmq, waits=[wtok3] + pbf.get(c, []), inc="pe")
            ev = k.emit("act", lambda e, c=c: e.copy(
                QT[:, c * 512:(c + 1) * 512], PB[c][:]),
                waits=[tk2] + qt_free, inc="act")
            pbf[c] = [ev]
            qevs.append(ev)
        w_free[b] = [tk2]
        qn = l2norm_apply(QT, o, qevs)
        sqs_free = qn[-1:]
        ot1 = k.emit("dve", lambda e, o=o: e.tensor_tensor(
            SCT, QT, BIG[:, o * T:(o + 1) * T], OP.mult),
            waits=qn + [scan_toks[o]], inc="dve")
        ot2 = k.emit("dve", lambda e, o=o: e.tensor_copy(
            BIG[:, o * T:(o + 1) * T], SCT), waits=[ot1], inc="dve")
        sct_free = [ot2]
        qt_free = [ot1]
        big_free2[o] = [ot2]
        if o == ND - 1:
            out_done = [ot2]

    YT = PT[:, 0:T]
    YS = PT[:, T:T + 512]
    yt_free = list(sct_free)
    ys_free = list(sqs_free)
    stat_mm_prev = []
    y_sts = []
    for o in range(ND):
        b = o % 2
        wtok = k.dma("pool", STRv[:, b * 16:(b + 1) * 16, :],
                     t["l1_woT"].rearrange("(a p) o2 -> p a o2", p=128)
                     [:, :, o * 128:(o + 1) * 128], waits=w_free[b])
        evs = []
        for c in range(4):
            def mmo(e, c=c, b=b):
                r = None
                for d in range(ND):
                    r = e.matmul(PB[c][:],
                                 STR[:, b * 2048 + d * 128: b * 2048 + (d + 1) * 128],
                                 BIG[:, d * T + c * 512: d * T + (c + 1) * 512],
                                 start=(d == 0), stop=(d == ND - 1))
                return r
            w = [wtok] + pbf.get(c, [])
            if o == 0 and c == 0:
                w += out_done
            tk = k.emit("pe", mmo, waits=w, inc="pe")
            ev = k.emit("act", lambda e, c=c: e.copy(
                YT[:, c * 512:(c + 1) * 512], PB[c][:]),
                waits=[tk] + yt_free, inc="act")
            pbf[c] = [ev]
            evs.append(ev)
            sq = k.emit("act", lambda e, c=c: e.activation(
                YS, YT[:, c * 512:(c + 1) * 512], AF.Square),
                waits=[ev] + ys_free, inc="act")
            mm = k.emit("pe", lambda e, c=c, o=o: (
                e.matmul(PB[4 + c][0:1, :], onec, YT[:, c * 512:(c + 1) * 512],
                         start=(o == 0), stop=(o == ND - 1)),
                e.matmul(PB[4 + c][32:33, :], onec, YS,
                         start=(o == 0), stop=(o == ND - 1)))[-1],
                waits=[sq] + stat_mm_prev, inc="pe")
            ys_free = [mm]
            stat_mm_prev = [mm]
        w_free[b] = [tk]
        st = k.dma("sp", t["sc_y"][:, o, :], YT, waits=evs + stat_mm_prev)
        yt_free = [st]
        y_sts.append(st)
    MUR = SMf[0:1, 0:T]
    RSR = SMf[0:1, T:2 * T]
    MSQ = SMf[0:1, 2 * T:2 * T + 512]
    fins = []
    for c in range(4):
        e1 = k.emit("dve", lambda e, c=c: e.tensor_scalar(
            MUR[:, c * 512:(c + 1) * 512], PB[4 + c][0:1, :], 1.0 / D, None,
            OP.mult, OP.bypass), waits=stat_mm_prev, inc="dve")
        e2 = k.emit("dve", lambda e, c=c: e.tensor_scalar(
            RSR[:, c * 512:(c + 1) * 512], PB[4 + c][32:33, :], 1.0 / D, None,
            OP.mult, OP.bypass), waits=[e1], inc="dve")
        e3 = k.emit("dve", lambda e, c=c: e.tensor_tensor(
            MSQ, MUR[:, c * 512:(c + 1) * 512], MUR[:, c * 512:(c + 1) * 512],
            OP.mult), waits=[e2], inc="dve")
        e4 = k.emit("dve", lambda e, c=c: e.tensor_tensor(
            RSR[:, c * 512:(c + 1) * 512], RSR[:, c * 512:(c + 1) * 512], MSQ,
            OP.subtract), waits=[e3], inc="dve")
        f1 = k.emit("dve", lambda e, c=c: e.tensor_scalar(
            RSR[:, c * 512:(c + 1) * 512], RSR[:, c * 512:(c + 1) * 512],
            LN_EPS, None, OP.add, OP.bypass), waits=[e4], inc="dve")
        f2 = k.emit("act", lambda e, c=c: e.sqrt(
            RSR[:, c * 512:(c + 1) * 512], RSR[:, c * 512:(c + 1) * 512]),
            waits=[f1], inc="act")
        f3 = k.emit("dve", lambda e, c=c: e.reciprocal(
            RSR[:, c * 512:(c + 1) * 512], RSR[:, c * 512:(c + 1) * 512]),
            waits=[f2], inc="dve")
        fins.append(f3)
    reps = []
    for c in range(4):
        tk = k.emit("pe", lambda e, c=c: (
            e.matmul(PB[c][:], oner_f, MUR[:, c * 512:(c + 1) * 512],
                     start=True, stop=True),
            e.matmul(PB[4 + c][:], oner_f, RSR[:, c * 512:(c + 1) * 512],
                     start=True, stop=True))[-1],
            waits=fins + pbf.get(c, []), inc="pe")
        reps.append(tk)
    YL = [STR[:, 4096 + i * 512: 4096 + (i + 1) * 512] for i in range(2)]
    TMP = SMf[:, 1024 + 512:1024 + 1024]
    S1T = PT[:, 2 * T:3 * T]
    yl_free = [[], []]
    tmp_free = []
    s1t_free = list(qt_free)
    st_prev = []
    for o in range(ND):
        evs = []
        for c in range(4):
            b = (o * 4 + c) % 2
            ld = k.dma("sp", YL[b], t["sc_y"][:, o, c * 512:(c + 1) * 512],
                       waits=yl_free[b] + (y_sts[-1:] if o == 0 and c < 2 else []))
            t1 = k.emit("dve", lambda e, b=b, c=c: e.tensor_tensor(
                TMP, YL[b], PB[c][:], OP.subtract),
                waits=[ld] + reps + tmp_free, inc="dve")
            t2 = k.emit("dve", lambda e, c=c: e.tensor_tensor(
                TMP, TMP, PB[4 + c][:], OP.mult), waits=[t1], inc="dve")
            ap = k.emit("act", lambda e, c=c, o=o: e.activation(
                S1T[:, c * 512:(c + 1) * 512], TMP, AF.Identity,
                bias=t["s_l1_ln_b"][:, o:o + 1],
                scale=t["s_l1_ln_w"][:, o:o + 1]),
                waits=[t2] + s1t_free, inc="act")
            tmp_free = [ap]
            yl_free[b] = [t2]
            evs.append(ap)
        st = k.dma("sp", t["sc_s1"][:, o, :], S1T, waits=evs)
        s1t_free = [st]
        st_prev = [st]
    return st_prev


def emit_phaseM(k, t, after):
    XN, BIG, SM, PT, STR = (t[n] for n in ("XN", "BIG", "SMALL", "POOLT", "STREAM"))
    PB = [t[f"PB{i}"] for i in range(8)]
    PTf = PT.ap().bitcast(F32)
    SMf = SM.ap().bitcast(F32)
    oner_f = t["ones_row_f32"][:]
    GTW = t["s_gateT"].ap()

    SG = SMf[:, 0:8]
    M1C = SMf[:, 8:9]
    M2C = SMf[:, 9:10]
    E8 = SMf[:, 16:24]
    WEM = SMf[0:8, 1024:3072]
    w_toks = []
    prev = list(after)
    for tt in range(NT):
        pb = PB[tt % 2]

        def mms(e, tt=tt, pb=pb):
            r = None
            for d in range(ND):
                r = e.matmul(pb[:, 0:8], XN[:, d * T + tt * 128: d * T + (tt + 1) * 128],
                             GTW[:, d * 8:(d + 1) * 8], start=(d == 0),
                             stop=(d == ND - 1))
            return r
        mm = k.emit("pe", mms, waits=prev + after, inc="pe")
        sg = k.emit("act", lambda e, pb=pb: e.activation(SG, pb[:, 0:8], AF.Sigmoid),
                    waits=[mm] + prev, inc="act")
        r1 = k.emit("dve", lambda e: e.reduce_max(M1C, SG, axis=AX.X),
                    waits=[sg], inc="dve")
        r2 = k.emit("dve", lambda e: e.tensor_scalar(
            E8, SG, M1C, None, OP.is_equal, OP.bypass), waits=[r1], inc="dve")
        r3 = k.emit("dve", lambda e: e.scalar_tensor_tensor(
            E8, E8, -1e30, SG, OP.mult, OP.add), waits=[r2], inc="dve")
        r4 = k.emit("dve", lambda e: e.reduce_max(M2C, E8, axis=AX.X),
                    waits=[r3], inc="dve")
        r5 = k.emit("dve", lambda e: e.tensor_scalar(
            E8, SG, M2C, None, OP.is_ge, OP.bypass), waits=[r4], inc="dve")
        r6 = k.emit("dve", lambda e: e.tensor_tensor(E8, E8, SG, OP.mult),
                    waits=[r5], inc="dve")
        tp = k.emit("pe", lambda e, tt=tt: e.transpose(
            PB[4][0:8, 0:128].bitcast(F32), E8.bitcast(F32), t["id_f32"][:]),
            waits=[r6] + w_toks[-1:], inc="pe")
        cw = k.emit("dve", lambda e, tt=tt: e.tensor_copy(
            WEM[:, tt * 128:(tt + 1) * 128], PB[4][0:8, 0:128]),
            waits=[tp], inc="dve")
        w_toks.append(cw)
        prev = [sg, r6, cw]
    wem_st = k.dma("sp", t["sc_wem"], WEM, waits=w_toks)

    HB = SM
    REP = PTf[:, 0:2048]
    TMP1 = PTf[:, 2048:2560]
    TMP2 = PTf[:, 2560:3072]
    SIL = PT[:, 6144:6656]
    ROW = SMf[0:1, 0:2048]
    W1S = STR[:, 0:2048]
    W3S = STR[:, 2048:4096]
    W2S = STR[:, 4096:5120]
    pbf = {c: [wem_st] + list(after) for c in range(8)}
    h_free = {f: [wem_st] for f in range(8)}
    rep_free = list(after)
    row_free = [wem_st]
    tmp_free = []
    sil_free = []
    w1_free = list(after)
    w3_free = list(after)
    w2_free = list(after)
    acc_free = {}
    last_w2mm = []

    for ei in range(NE + 1):
        shared = (ei == 0)
        e_idx = ei - 1
        if shared:
            w1v = t["moe_sw1T"].rearrange("(a p) f -> p a f", p=128)
            w3v = t["moe_sw3T"].rearrange("(a p) f -> p a f", p=128)
            w2v = t["moe_sw2T"].rearrange("(a p) f -> p a f", p=128)
        else:
            w1v = t["moe_w1T"][e_idx].rearrange("(a p) f -> p a f", p=128)
            w3v = t["moe_w3T"][e_idx].rearrange("(a p) f -> p a f", p=128)
            w2v = t["moe_w2T"][e_idx].rearrange("(a p) f -> p a f", p=128)
        rep_toks = []
        if not shared:
            rl = k.dma("sp", ROW, t["sc_wem"][e_idx:e_idx + 1, :], waits=row_free)
            cps = []
            for c in range(4):
                mm = k.emit("pe", lambda e, c=c: e.matmul(
                    PB[0][:], oner_f, ROW[:, c * 512:(c + 1) * 512],
                    start=True, stop=True),
                    waits=[rl] + (cps[-1:] if cps else rep_free + pbf.get(0, [])),
                    inc="pe")
                cp = k.emit("dve", lambda e, c=c: e.tensor_copy(
                    REP[:, c * 512:(c + 1) * 512], PB[0][:]), waits=[mm],
                    inc="dve")
                cps.append(cp)
            rep_toks = cps
            row_free = [cps[0]]
            pbf[0] = [cps[3]]
        h_toks = []
        for f in range(8):
            l1 = k.dma("pool", W1S.rearrange("p (a z) -> p a z", a=16),
                       w1v[:, :, f * 128:(f + 1) * 128], waits=w1_free)
            l3 = k.dma("pool", W3S.rearrange("p (a z) -> p a z", a=16),
                       w3v[:, :, f * 128:(f + 1) * 128], waits=w3_free)
            z1t, z3t = [], []
            for c in range(4):
                def mz(e, c=c, off=0, WS=W1S):
                    r = None
                    for d in range(ND):
                        r = e.matmul(PB[off + c][:],
                                     WS[:, d * 128:(d + 1) * 128],
                                     XN[:, d * T + c * 512: d * T + (c + 1) * 512],
                                     start=(d == 0), stop=(d == ND - 1))
                    return r
                z1t.append(k.emit("pe", lambda e, c=c: mz(e, c, 0, W1S),
                                  waits=[l1] + pbf.get(c, []) + rep_toks,
                                  inc="pe"))
            for c in range(4):
                z3t.append(k.emit("pe", lambda e, c=c: mz(e, c, 4, W3S),
                                  waits=[l3] + pbf.get(4 + c, []), inc="pe"))
            w1_free = [z1t[-1]]
            w3_free = [z3t[-1]]
            hts = []
            for c in range(4):
                if shared:
                    s1 = k.emit("act", lambda e, c=c: e.activation(
                        SIL, PB[c][:], AF.Silu), waits=[z1t[c]] + sil_free,
                        inc="act")
                    ht = k.emit("dve", lambda e, c=c, f=f: e.tensor_tensor(
                        HB[:, f * T + c * 512: f * T + (c + 1) * 512], SIL,
                        PB[4 + c][:], OP.mult),
                        waits=[s1, z3t[c]] + h_free.get(f, []), inc="dve")
                    sil_free = [ht]
                    pbf[c] = [s1]
                    pbf[4 + c] = [ht]
                else:
                    a1 = k.emit("dve", lambda e, c=c: e.tensor_tensor(
                        TMP1, PB[c][:], REP[:, c * 512:(c + 1) * 512], OP.mult),
                        waits=[z1t[c]] + tmp_free, inc="dve")
                    s1 = k.emit("act", lambda e: e.activation(
                        SIL, TMP1, AF.Silu), waits=[a1] + sil_free, inc="act")
                    a2 = k.emit("dve", lambda e, c=c: e.tensor_tensor(
                        TMP2, PB[4 + c][:], REP[:, c * 512:(c + 1) * 512],
                        OP.mult), waits=[z3t[c]] + (tmp_free if False else [a1]),
                        inc="dve")
                    ht = k.emit("dve", lambda e, c=c, f=f: e.tensor_tensor(
                        HB[:, f * T + c * 512: f * T + (c + 1) * 512], SIL,
                        TMP2, OP.mult), waits=[s1, a2] + h_free.get(f, []),
                        inc="dve")
                    tmp_free = [ht]
                    sil_free = [ht]
                    pbf[c] = [a1]
                    pbf[4 + c] = [a2]
                hts.append(ht)
            h_toks += hts
            h_free[f] = hts
        for o in range(ND):
            lw = k.dma("pool", W2S.rearrange("p (a z) -> p a z", a=8),
                       w2v[:, :, o * 128:(o + 1) * 128], waits=w2_free)
            mm_toks = []
            for c in range(4):
                pb = PB[(o % 2) * 4 + c]

                def mw(e, c=c, pb=pb):
                    r = None
                    for ff in range(8):
                        r = e.matmul(pb[:], W2S[:, ff * 128:(ff + 1) * 128],
                                     HB[:, ff * T + c * 512: ff * T + (c + 1) * 512],
                                     start=(ff == 0), stop=(ff == 7))
                    return r
                w = [lw] + pbf.get((o % 2) * 4 + c, [])
                if o == 0:
                    w += h_toks[-1:]
                tk = k.emit("pe", mw, waits=w, inc="pe")
                mm_toks.append(tk)
                if shared:
                    ev = k.emit("act", lambda e, c=c, o=o, pb=pb: e.copy(
                        BIG[:, o * T + c * 512: o * T + (c + 1) * 512], pb[:]),
                        waits=[tk] + acc_free.get((o, c), []), inc="act")
                else:
                    ev = k.emit("dve", lambda e, c=c, o=o, pb=pb: e.tensor_tensor(
                        BIG[:, o * T + c * 512: o * T + (c + 1) * 512],
                        BIG[:, o * T + c * 512: o * T + (c + 1) * 512],
                        pb[:], OP.add),
                        waits=[tk] + acc_free.get((o, c), []), inc="dve")
                pbf[(o % 2) * 4 + c] = [ev]
                acc_free[(o, c)] = [ev]
            w2_free = mm_toks
            last_w2mm = mm_toks
        for f in range(8):
            h_free[f] = list(last_w2mm)
        row_free = list(last_w2mm)
    return [ev]


def emit_phaseF(k, t, after):
    XN, BIG, SM, PT, STR = (t[n] for n in ("XN", "BIG", "SMALL", "POOLT", "STREAM"))
    G1 = [SM[:, i * T:(i + 1) * T] for i in range(2)]
    S1L = [SM[:, (2 + i) * T:(3 + i) * T] for i in range(2)]
    X1L = [SM[:, (4 + i) * T:(5 + i) * T] for i in range(2)]
    TM = [SM[:, (6 + i) * T:(7 + i) * T] for i in range(2)]
    g_free = [list(after), list(after)]
    s_free = [list(after), list(after)]
    x_free = [list(after), list(after)]
    tm_free = [[], []]
    xn_toks = []
    for d in range(ND):
        b = d % 2
        lds = k.dma("sp", S1L[b], t["sc_s1"][:, d, :], waits=s_free[b])
        ldx = k.dma("sp", X1L[b], t["sc_x1"][:, d, :], waits=x_free[b])
        g1 = k.emit("act", lambda e, d=d, b=b: e.activation(
            G1[b], BIG[:, d * T:(d + 1) * T], AF.Silu),
            waits=g_free[b] + after, inc="act")
        t1 = k.emit("dve", lambda e, b=b: e.tensor_tensor(
            TM[b], S1L[b], G1[b], OP.mult), waits=[lds, g1] + tm_free[b],
            inc="dve")
        x2 = k.emit("dve", lambda e, b=b, d=d: e.tensor_tensor(
            XN[:, d * T:(d + 1) * T], TM[b], X1L[b], OP.add),
            waits=[t1, ldx] + after[:1 if d == 0 else 0], inc="dve")
        g_free[b] = [t1]
        s_free[b] = [t1]
        x_free[b] = [x2]
        tm_free[b] = [x2]
        xn_toks.append(x2)
    big_tok = t["BIG"].ap().rearrange("p (tt z) -> p tt z", tt=NT)
    out_toks = []
    for d in range(ND):
        xp = k.xpose(big_tok[:, :, d * 128:(d + 1) * 128],
                     XN[:, d * T:(d + 1) * T], waits=xn_toks if d == 0 else [])
        out_toks.append(xp)
    last_xp = out_toks[-1]
    for tt in range(NT):
        k.dma("pool", t["out"][tt * 128:(tt + 1) * 128, :],
              BIG[:, tt * T:(tt + 1) * T], waits=[last_xp])
    return []


ALL_PHASES = ("A", "B", "Wo", "C", "D", "E", "M", "F")


def build(phases=ALL_PHASES, dump_sc=(), dump_sb=()):
    if not hasattr(build, 'b_groups'):
        build.b_groups, build.b_heads = range(4), range(4)
    alloc.dump = tuple(dump_sc)
    nc = bass.Bass("TRN2", target_bir_lowering=False)
    with contextlib.ExitStack() as st:
        t = alloc(nc, st)
        k = KB(nc)
        emit_consts(k, t)
        if "A" in phases:
            emit_phaseA(k, t)
        if "B" in phases:
            oh_last, pbf = emit_phaseB(k, t, groups=build.b_groups, heads=build.b_heads)
            if "Wo" in phases:
                emit_phaseWo(k, t, oh_last, pbf)
        after = [(s, k.v(s)) for s in SEMS]
        if "C" in phases:
            ctoks = emit_phaseC(k, t, after)
            after = [(s, k.v(s)) for s in SEMS]
        if "D" in phases:
            emit_phaseD(k, t, after)
            after = [(s, k.v(s)) for s in SEMS]
        if "E" in phases:
            emit_phaseE(k, t, after)
            after = [(s, k.v(s)) for s in SEMS]
        if "M" in phases:
            emit_phaseM(k, t, after)
            after = [(s, k.v(s)) for s in SEMS]
        if "F" in phases:
            emit_phaseF(k, t, after)
            after = [(s, k.v(s)) for s in SEMS]
        for nm in dump_sb:
            ap = t[nm].ap()
            dt = nc.dram_tensor("dump_" + nm, list(ap.shape),
                                ap.dtype, kind="ExternalOutput").ap()
            k.dma("sp", dt, t[nm][:],
                  waits=[(s, k.v(s)) for s in ("pe", "act", "dve", "pool")])
        fin = [(s, k.v(s)) for s in SEMS]
        k.emit("sp", lambda e: e.nop(), waits=fin)
        S = {s: st.enter_context(nc.semaphore(s)) for s in SEMS}
        k.S = S
        block = st.enter_context(nc.Block())
        for eng, attr in (("pe", "tensor"), ("act", "scalar"), ("dve", "vector"),
                          ("pool", "gpsimd"), ("sp", "sync")):
            def make(fns):
                def run(e):
                    for f in fns:
                        f(e)
                return run
            getattr(block, attr)(make(k.ops[eng]))
    return nc



def kernel(**inputs):
    maps = prep_inputs(inputs)
    nc = build()
    res = run_bass_kernel_spmd(nc, maps, core_ids=list(range(8)))
    out = np.stack([np.asarray(res.results[c]["out"]) for c in range(8)])
    return out.astype(np.float32)
